# version 02
# baseline (speedup 1.0000x reference)
import numpy as np
import ml_dtypes
from contextlib import ExitStack

import concourse.bass as bass
import concourse.mybir as mybir
from concourse.bass_utils import run_bass_kernel_spmd
from concourse.bass import IndirectOffsetOnAxis

F32 = mybir.dt.float32
BF16 = mybir.dt.bfloat16
I32 = mybir.dt.int32
U32 = mybir.dt.uint32
AF = mybir.ActivationFunctionType
ALU = mybir.AluOpType
AX = mybir.AxisListType

BF = ml_dtypes.bfloat16

B, T, E = 2, 8192, 1024
NH, DH = 16, 64
NLOC, NLSH = 4, 12
BK, R, NB = 64, 4, 128
FF = 4096
EX, K3, HH = 12, 3, 4096
TOKS = T * B // 8
NT = TOKS // 512
EPS = 1e-5
TSELF = -50000.0
NEGBIG = -3.0e38


class Prog:

    def __init__(self, nc, st, names):
        self.nc = nc
        self.sem = {}
        self.cnt = {}
        for n in names:
            self.sem[n] = st.enter_context(nc.semaphore("sem_" + n))
            self.cnt[n] = 0

    def inc(self, name, ins, n=1):
        ins.then_inc(self.sem[name], n)
        self.cnt[name] += n
        return (name, self.cnt[name])

    def tok(self, name):
        return (name, self.cnt[name])

    def wait(self, eng, tok):
        if tok is not None:
            eng.wait_ge(self.sem[tok[0]], tok[1])

    def dma(self, eng, name, out, in_, **kw):
        if self.cnt[name] > 0:
            eng.wait_ge(self.sem[name], self.cnt[name])
        n0 = self.nc.n_instructions()
        ins = eng.dma_start(out, in_, **kw)
        k = max(1, self.nc.n_instructions() - n0)
        ins.then_inc(self.sem[name], 16)
        self.cnt[name] += 16 * k
        return (name, self.cnt[name])



def build_L1():
    nc = bass.Bass("TRN2", target_bir_lowering=False)
    xT = nc.dram_tensor("xT", [E, TOKS], F32, kind="ExternalInput").ap()
    wqkT = nc.dram_tensor("wqkT", [8, 128, 8, 128], F32, kind="ExternalInput").ap()
    wvT = nc.dram_tensor("wvT", [8, 128, 8, 128], F32, kind="ExternalInput").ap()
    rotw = nc.dram_tensor("rotw", [128, 128], F32, kind="ExternalInput").ap()
    ones_col = nc.dram_tensor("ones_col", [128, 1], F32, kind="ExternalInput").ap()
    ones_row = nc.dram_tensor("ones_row", [1, 128], F32, kind="ExternalInput").ap()
    iobB = nc.dram_tensor("iobB", [128, 64], F32, kind="ExternalInput").ap()
    headmask = nc.dram_tensor("headmask", [128, 8, 16], F32, kind="ExternalInput").ap()
    headexp = nc.dram_tensor("headexp", [16, 8, 128], F32, kind="ExternalInput").ap()
    qkT_o = nc.dram_tensor("qkT", [E, TOKS], BF16, kind="ExternalOutput").ap()
    khT_o = nc.dram_tensor("khT", [E, TOKS], BF16, kind="ExternalOutput").ap()
    vT_o = nc.dram_tensor("vT", [E, TOKS], BF16, kind="ExternalOutput").ap()
    buck_o = nc.dram_tensor("buckO", [128, NLSH, 16, R], F32, kind="ExternalOutput").ap()

    st = ExitStack()
    p = Prog(nc, st, ["T", "V", "S", "gd", "gw0", "gw1", "yd", "yx0", "yx1"])
    t, v, s, g, y = nc.tensor, nc.vector, nc.scalar, nc.gpsimd, nc.sync

    xTb = st.enter_context(nc.sbuf_tensor("xTb", [128, 8, TOKS], BF16))
    vstage = st.enter_context(nc.sbuf_tensor("vstage", [128, 2, 512], BF16))
    sq = st.enter_context(nc.sbuf_tensor("sq", [128, 2, 512], BF16))
    qkT = st.enter_context(nc.sbuf_tensor("qkT_sb", [128, 8, TOKS], BF16))
    khT = st.enter_context(nc.sbuf_tensor("khT_sb", [128, 8, TOKS], BF16))
    xf = st.enter_context(nc.sbuf_tensor("xf_sb", [128, 2, TOKS], F32))
    wqf = st.enter_context(nc.sbuf_tensor("wqf_sb", [128, 2, 8, 128], F32))
    wvb = st.enter_context(nc.sbuf_tensor("wvb_sb", [128, 2, 8, 128], BF16))
    qf32 = st.enter_context(nc.sbuf_tensor("qf32", [128, TOKS], F32))
    rbc = st.enter_context(nc.sbuf_tensor("rbc", [128, 512], F32))
    frL = st.enter_context(nc.sbuf_tensor("frL", [128, 4, R, 64], F32))
    mx = st.enter_context(nc.sbuf_tensor("mx", [128, 16], F32))
    bucksb = st.enter_context(nc.sbuf_tensor("bucksb", [128, NLSH, 16, R], F32))
    cwrow = st.enter_context(nc.sbuf_tensor("cwrow", [1, 128], F32))
    cwrb = st.enter_context(nc.sbuf_tensor("cwrb", [1, 128], BF16))
    c1 = st.enter_context(nc.sbuf_tensor("c_ones_col", [128, 1], BF16))
    c1f = st.enter_context(nc.sbuf_tensor("c_ones_colf", [128, 1], F32))
    r1 = st.enter_context(nc.sbuf_tensor("c_ones_row", [1, 128], BF16))
    r1f = st.enter_context(nc.sbuf_tensor("c_ones_rowf", [1, 128], F32))
    iob = st.enter_context(nc.sbuf_tensor("c_iob", [128, 64], F32))
    rwsb = st.enter_context(nc.sbuf_tensor("c_rotw", [128, 128], F32))
    hm = st.enter_context(nc.sbuf_tensor("c_headmask", [128, 8, 16], BF16))
    hx = st.enter_context(nc.sbuf_tensor("c_headexp", [16, 8, 128], BF16))
    stats = st.enter_context(nc.sbuf_tensor("stats", [1, 3, 512], F32))
    mrow = st.enter_context(nc.sbuf_tensor("mrow", [1, TOKS], BF16))
    rrow = st.enter_context(nc.sbuf_tensor("rrow", [1, TOKS], BF16))
    munf = st.enter_context(nc.sbuf_tensor("munf", [1, TOKS], F32))
    rsf = st.enter_context(nc.sbuf_tensor("rsf", [1, TOKS], F32))
    rn2 = st.enter_context(nc.sbuf_tensor("rn2", [16, TOKS], F32))
    rnr = st.enter_context(nc.sbuf_tensor("rnr", [16, TOKS], BF16))
    ps = [st.enter_context(nc.psum_tensor(f"ps{i}", [128, 512], F32)) for i in range(8)]

    p.dma(g, "gd", xTb[:], xT.rearrange("(c p) n -> p c n", p=128))
    p.dma(g, "gd", c1[:], ones_col[:, :])
    p.dma(g, "gd", c1f[:], ones_col[:, :])
    p.dma(g, "gd", r1[:], ones_row[:, :])
    p.dma(g, "gd", r1f[:], ones_row[:, :])
    p.dma(g, "gd", iob[:], iobB[:, :])
    p.dma(g, "gd", rwsb[:], rotw[:, :])
    p.dma(g, "gd", hm[:], headmask[:, :, :])
    ld_all = p.dma(g, "gd", hx[:], headexp[:, :, :])

    p.wait(t, ld_all)
    p.wait(v, ld_all)
    mm_sq = {}
    for nt in range(NT):
        sl = slice(512 * nt, 512 * nt + 512)
        for c in range(8):
            sb = c % 2
            if (nt * 8 + c) >= 2:
                p.wait(v, mm_sq[nt * 8 + c - 2])
            sqd = p.inc("V", v.tensor_tensor(sq[:, sb, :], xTb[:, c, sl], xTb[:, c, sl], ALU.mult))
            t.matmul(ps[6][0:1, :], c1[:, :], xTb[:, c, sl], start=(c == 0), stop=(c == 7))
            p.wait(t, sqd)
            mm_sq[nt * 8 + c] = mmt = p.inc(
                "T", t.matmul(ps[7][0:1, :], c1[:, :], sq[:, sb, :],
                              start=(c == 0), stop=(c == 7)))
        p.wait(v, mmt)
        a = p.inc("V", v.tensor_scalar_mul(munf[:, sl], ps[6][0:1, :], -1.0 / E))
        p.wait(v, a)
        a = p.inc("V", v.tensor_copy(mrow[:, sl], munf[:, sl]))
        p.wait(v, a)
        a = p.inc("V", v.tensor_tensor(stats[:, 2, :], munf[:, sl], munf[:, sl], ALU.mult))
        p.wait(v, a)
        a = p.inc("V", v.scalar_tensor_tensor(stats[:, 1, :], ps[7][0:1, :], 1.0 / E,
                                              stats[:, 2, :], ALU.mult, ALU.subtract))
        p.wait(v, a)
        a = p.inc("V", v.tensor_scalar_add(stats[:, 1, :], stats[:, 1, :], EPS))
        p.wait(s, a)
        a = p.inc("S", s.activation(stats[:, 2, :], stats[:, 1, :], AF.Sqrt))
        p.wait(v, a)
        a = p.inc("V", v.reciprocal(rsf[:, sl], stats[:, 2, :]))
        p.wait(v, a)
        a = p.inc("V", v.tensor_copy(rrow[:, sl], rsf[:, sl]))
        p.wait(t, a)
    stats_done = p.tok("V")

    p.wait(y, ld_all)
    wld = {0: p.dma(g, "gw0", wqf[:, 0, :, :], wqkT[0, :, :, :])}
    xld = {}
    p.wait(t, stats_done)
    for fo in range(8):
        if fo + 1 < 8:
            p.wait(g, p.tok("T"))
            p.wait(g, p.tok("gw" + str((fo + 1) % 2)))
            wld[fo + 1] = p.dma(g, "gw" + str((fo + 1) % 2), wqf[:, (fo + 1) % 2, :, :],
                                wqkT[fo + 1, :, :, :])
        p.wait(t, wld[fo])
        for fi in range(8):
            mmc = p.inc("T", t.matmul(ps[4][0:1, 0:128], c1f[:, :], wqf[:, fo % 2, fi, :],
                                      start=(fi == 0), stop=(fi == 7)))
        p.wait(v, mmc)
        cwd = p.inc("V", v.tensor_copy(cwrow[:, :], ps[4][0:1, 0:128]))
        p.wait(y, p.tok("T"))
        p.wait(y, p.tok("yx0"))
        xld[0] = p.dma(y, "yx0", xf[:, 0, :], xT[0:128, :])
        for fi in range(8):
            if fi + 1 < 8:
                p.wait(y, p.tok("T"))
                p.wait(y, p.tok("yx" + str((fi + 1) % 2)))
                xld[fi + 1] = p.dma(y, "yx" + str((fi + 1) % 2), xf[:, (fi + 1) % 2, :],
                                    xT[128 * (fi + 1):128 * (fi + 2), :])
            p.wait(t, xld[fi])
            for nt in range(NT):
                mm = p.inc("T", t.matmul(ps[nt][:, :], wqf[:, fo % 2, fi, :],
                                         xf[:, fi % 2, 512 * nt:512 * nt + 512],
                                         start=(fi == 0), stop=False))
        p.wait(t, cwd)
        for nt in range(NT):
            mm = p.inc("T", t.matmul(ps[nt][:, :], cwrow[:, :],
                                     munf[:, 512 * nt:512 * nt + 512],
                                     start=False, stop=True))
        ev = None
        for nt in range(NT):
            sl = slice(512 * nt, 512 * nt + 512)
            bc = p.inc("T", t.matmul(ps[4][:, :], r1f[:, :], rsf[:, sl], start=True, stop=True))
            p.wait(v, bc)
            a = p.inc("V", v.tensor_copy(rbc[:, :], ps[4][:, :]))
            p.wait(v, a)
            a = p.inc("V", v.tensor_tensor(qkT[:, fo, sl], ps[nt][:, :], rbc[:, :], ALU.mult))
            if fo >= 2:
                a = p.inc("V", v.tensor_tensor(qf32[:, sl], ps[nt][:, :], rbc[:, :], ALU.mult))
            p.wait(t, a)
            ev = a
        if fo >= 2:
            p.wait(t, ev)
            for hh in range(2):
                u = 2 * fo + hh - 4
                for q in range(4):
                    for sub in range(4):
                        j = 4 * q + sub
                        mm = p.inc("T", t.matmul(
                            ps[5][:, 128 * sub:128 * sub + 128],
                            qf32[64 * hh:64 * hh + 64, 128 * j:128 * j + 128],
                            rwsb[64 * hh:64 * hh + 64, :], start=(sub == 0),
                            stop=True, skip_group_check=True))
                    p.wait(v, mm)
                    a1 = p.inc("V", v.tensor_copy(
                        frL[:, :, :, 0:32],
                        ps[5][:, :].rearrange("p (s r i) -> p s r i", s=4, r=R)))
                    p.wait(v, a1)
                    a2 = p.inc("V", v.tensor_scalar_mul(
                        frL[:, :, :, 32:64],
                        ps[5][:, :].rearrange("p (s r i) -> p s r i", s=4, r=R),
                        -1.0))
                    p.wait(v, a2)
                    frv = frL[:].rearrange("p s r i -> p (s r) i")
                    a = p.inc("V", v.tensor_reduce(mx[:, :], frv, AX.X, ALU.max))
                    p.wait(v, a)
                    a = p.inc("V", v.tensor_tensor(
                        frv, frv, mx[:, :].unsqueeze(2).to_broadcast([128, 16, 64]),
                        ALU.is_ge))
                    p.wait(v, a)
                    a = p.inc("V", v.scalar_tensor_tensor(
                        frv, frv, -1.0e6,
                        iob[:].unsqueeze(1).to_broadcast([128, 16, 64]),
                        ALU.mult, ALU.add))
                    p.wait(v, a)
                    a = p.inc("V", v.tensor_reduce(
                        bucksb[:, u, 4 * q:4 * q + 4, :],
                        frv.rearrange("p (s r) i -> p s r i", s=4), AX.X, ALU.min))
                    p.wait(t, a)
    qk_done = p.tok("V")
    p.wait(y, qk_done)
    bo = p.dma(y, "yd", buck_o[:, :, :, :], bucksb[:])

    p.wait(g, p.tok("gw0"))
    p.wait(g, p.tok("gw1"))
    wld = {0: p.dma(g, "gw0", wvb[:, 0, :, :], wvT[0, :, :, :])}
    for fo in range(8):
        if fo + 1 < 8:
            p.wait(g, p.tok("T"))
            p.wait(g, p.tok("gw" + str((fo + 1) % 2)))
            wld[fo + 1] = p.dma(g, "gw" + str((fo + 1) % 2), wvb[:, (fo + 1) % 2, :, :],
                                wvT[fo + 1, :, :, :])
        p.wait(t, wld[fo])
        for fi in range(8):
            mmc = p.inc("T", t.matmul(ps[4][0:1, 0:128], c1[:, :], wvb[:, fo % 2, fi, :],
                                      start=(fi == 0), stop=(fi == 7)))
        p.wait(v, mmc)
        cwd = p.inc("V", v.tensor_copy(cwrb[:, :], ps[4][0:1, 0:128]))
        for fi in range(8):
            for nt in range(NT):
                mm = p.inc("T", t.matmul(ps[nt][:, :], wvb[:, fo % 2, fi, :],
                                         xTb[:, fi, 512 * nt:512 * nt + 512],
                                         start=(fi == 0), stop=False))
        p.wait(t, cwd)
        for nt in range(NT):
            mm = p.inc("T", t.matmul(ps[nt][:, :], cwrb[:, :],
                                     mrow[:, 512 * nt:512 * nt + 512],
                                     start=False, stop=True))
        for nt in range(NT):
            sl = slice(512 * nt, 512 * nt + 512)
            bc = p.inc("T", t.matmul(ps[4][:, :], r1[:, :], rrow[:, sl], start=True, stop=True))
            p.wait(v, bc)
            a = p.inc("V", v.tensor_copy(rbc[:, :], ps[4][:, :]))
            p.wait(v, a)
            p.wait(v, p.tok("yd"))
            a = p.inc("V", v.tensor_tensor(vstage[:, nt % 2, :], ps[nt][:, :], rbc[:, :], ALU.mult))
            p.wait(t, a)
            p.wait(y, a)
            p.dma(y, "yd", vT_o.rearrange("(c p) n -> p c n", p=128)[:, fo, sl],
                  vstage[:, nt % 2, :])
    v_done = p.tok("yd")

    p.wait(v, qk_done)
    mm_sq2 = {}
    for nt in range(NT):
        sl = slice(512 * nt, 512 * nt + 512)
        for c in range(8):
            sb = c % 2
            if (nt * 8 + c) >= 2:
                p.wait(v, mm_sq2[nt * 8 + c - 2])
            sqd = p.inc("V", v.tensor_tensor(sq[:, sb, :], qkT[:, c, sl], qkT[:, c, sl], ALU.mult))
            p.wait(t, sqd)
            mm_sq2[nt * 8 + c] = mmt = p.inc(
                "T", t.matmul(ps[6][0:16, :], hm[:, c, :], sq[:, sb, :],
                              start=(c == 0), stop=(c == 7)))
        p.wait(s, mmt)
        ev = p.inc("S", s.copy(rn2[:, sl], ps[6][0:16, :]))
        p.wait(t, ev)
    nrm = p.tok("S")
    p.wait(s, nrm)
    a = p.inc("S", s.activation(rn2[:, :], rn2[:, :], AF.Sqrt))
    p.wait(v, a)
    a = p.inc("V", v.reciprocal(rn2[:, :], rn2[:, :]))
    p.wait(v, a)
    rnr_done = p.inc("V", v.tensor_copy(rnr[:, :], rn2[:, :]))
    p.wait(t, rnr_done)
    for nt in range(NT):
        sl = slice(512 * nt, 512 * nt + 512)
        for c in range(8):
            bank = 5 + (c % 2)
            bc = p.inc("T", t.matmul(ps[bank][:, :], hx[:, c, :], rnr[:, sl], start=True, stop=True))
            p.wait(v, bc)
            dn = p.inc("V", v.tensor_tensor(khT[:, c, sl], qkT[:, c, sl], ps[bank][:, :], ALU.mult))
            p.wait(t, dn)
    kh_done = p.tok("V")

    p.wait(y, qk_done)
    o1 = p.dma(y, "yd", qkT_o.rearrange("(c p) n -> p c n", p=128), qkT[:])
    p.wait(y, kh_done)
    o2 = p.dma(y, "yd", khT_o.rearrange("(c p) n -> p c n", p=128), khT[:])
    p.wait(y, v_done)
    p.wait(y, p.tok("yd"))
    st.close()
    return nc



def l1_consts():
    hm = np.zeros((128, 8, 16), np.float32)
    hx = np.zeros((16, 8, 128), np.float32)
    for c in range(8):
        for pp in range(128):
            h = (128 * c + pp) // 64
            hm[pp, c, h] = 1.0
            hx[h, c, pp] = 1.0
    return {
        "ones_col": np.ones((128, 1), np.float32),
        "ones_row": np.ones((1, 128), np.float32),
        "iobB": np.broadcast_to(np.arange(64, dtype=np.float32) + 1.0e6,
                                (128, 64)).copy(),
        "headmask": hm,
        "headexp": hx,
    }


def l1_inputs(x, w_qk, w_v, rotations):
    consts = l1_consts()
    def blk(w):
        wt = w.T.reshape(8, 128, 8, 128)
        return np.ascontiguousarray(wt.transpose(2, 1, 0, 3))
    wqkT = blk(w_qk)
    wvT = blk(w_v)
    r2 = rotations.reshape(DH, 128)
    rotw = np.ascontiguousarray(np.concatenate([r2, r2], axis=0))
    maps = []
    for c in range(8):
        b, j = divmod(c, 4)
        xs = x[b, TOKS * j:TOKS * (j + 1), :]
        maps.append(dict(consts, xT=np.ascontiguousarray(xs.T), wqkT=wqkT,
                         wvT=wvT, rotw=rotw))
    return maps




NLU = 3
LTOT = R * T
EPSSELF = 1e-30


def build_L2():
    nc = bass.Bass("TRN2", target_bir_lowering=False)

    def I(name, shape, dt=F32):
        return nc.dram_tensor(name, shape, dt, kind="ExternalInput").ap()

    src_in = [I(f"src{u}", [T, 256], BF16) for u in range(NLU)]
    buck_in = [I(f"buck{u}", [128, 64, R]) for u in range(NLU)]
    locq_in = I("locq", [DH, T], BF16)
    lock_in = I("lock", [DH, T], BF16)
    locv_in = I("locv", [128, 64, DH + 1], BF16)
    ones_col = I("ones_col", [128, 1])
    iotaB = I("iotaB", [128, 64])
    lincT = I("lincT", [128, 128])
    l64sT = I("l64sT", [64, 64])
    gfc_in = I("gfc", [2, 128])
    auxc_in = I("auxc", [2, 128])
    auxc2_in = I("auxc2", [2, 64])
    epsid_in = I("epsid", [128, 64])
    epsi128_in = I("epsi128", [128, 128])
    trim_in = I("trimask", [128, 128])
    oarr = [nc.dram_tensor(f"oarr{u}", [LTOT + 64, 66], F32, kind="ExternalOutput").ap()
            for u in range(NLU)]
    pirk_o = nc.dram_tensor("pirk", [128, NLU, R, 64], F32, kind="ExternalOutput").ap()
    loco_o = nc.dram_tensor("loco", [128, 64, DH], BF16, kind="ExternalOutput").ap()
    arr = nc.dram_tensor("arr", [LTOT + 64, 256], BF16, kind="Internal").ap()
    tmpc = nc.dram_tensor("tmpc", [64, 64], BF16, kind="Internal").ap()
    tmpp = nc.dram_tensor("tmpp", [2, 64], F32, kind="Internal").ap()
    tmpg = nc.dram_tensor("tmpg", [2, 64, 64], F32, kind="Internal").ap()

    st = ExitStack()
    p = Prog(nc, st, ["T", "V", "S", "gd", "gs", "yd", "yl", "yc"])
    t, v, s, g, y = nc.tensor, nc.vector, nc.scalar, nc.gpsimd, nc.sync

    c1 = st.enter_context(nc.sbuf_tensor("c1", [128, 1], BF16))
    iob = st.enter_context(nc.sbuf_tensor("iob", [128, 64], F32))
    linc = st.enter_context(nc.sbuf_tensor("linc", [128, 128], BF16))
    l64 = st.enter_context(nc.sbuf_tensor("l64", [64, 64], BF16))
    gfc = st.enter_context(nc.sbuf_tensor("gfc_s", [2, 128], BF16))
    auxc = st.enter_context(nc.sbuf_tensor("auxc_s", [2, 128], BF16))
    auxc2 = st.enter_context(nc.sbuf_tensor("auxc2_s", [2, 64], BF16))
    epsid = st.enter_context(nc.sbuf_tensor("epsid_s", [128, 64], BF16))
    epsi128 = st.enter_context(nc.sbuf_tensor("epsi128_s", [128, 128], BF16))
    trim = st.enter_context(nc.sbuf_tensor("trim", [128, 128], BF16))
    zrow = st.enter_context(nc.sbuf_tensor("zrow", [64, 256], BF16))
    sdb1 = st.enter_context(nc.sbuf_tensor("sdb1", [128, 8320], BF16))
    sdb2 = st.enter_context(nc.sbuf_tensor("sdb2", [128, 8320], BF16))
    svb = st.enter_context(nc.sbuf_tensor("svb", [64, 130, DH + 1], BF16))
    ez = [st.enter_context(nc.sbuf_tensor(f"ez{i}", [128, 512], BF16)) for i in range(2)]
    stg = st.enter_context(nc.sbuf_tensor("stg", [64, 64, 66], F32))
    rsq = st.enter_context(nc.sbuf_tensor("rsq", [64, 4], F32))
    rsq2 = st.enter_context(nc.sbuf_tensor("rsq2", [64, 4], F32))
    rdot = st.enter_context(nc.sbuf_tensor("rdot", [128, 64], F32))
    P = [st.enter_context(nc.psum_tensor(f"p{i}", [128, 512], F32)) for i in range(6)]
    pS = st.enter_context(nc.psum_tensor("pS", [128, 512], F32))

    p.dma(g, "gd", c1[:], ones_col[:, :])
    p.dma(g, "gd", iob[:], iotaB[:, :])
    p.dma(g, "gd", linc[:], lincT[:, :])
    p.dma(g, "gd", l64[:], l64sT[:, :])
    p.dma(g, "gd", gfc[:], gfc_in[:, :])
    p.dma(g, "gd", auxc[:], auxc_in[:, :])
    p.dma(g, "gd", auxc2[:], auxc2_in[:, :])
    p.dma(g, "gd", epsid[:], epsid_in[:, :])
    p.dma(g, "gd", epsi128[:], epsi128_in[:, :])
    ldc = p.dma(g, "gd", trim[:], trim_in[:, :])
    zr = p.inc("V", v.memset(zrow[:], 0.0))
    p.inc("V", v.memset(svb[0:64, :, 64:65], 1.0))
    p.inc("V", v.memset(stg[:, :, 65:66], 0.0))
    p.wait(y, zr)
    p.dma(y, "yd", arr[LTOT:LTOT + 64, :], zrow[:, :])

    with ExitStack() as stA:
        SR = stA.enter_context(nc.sbuf_tensor("SR", [128, 64, 256], BF16, side="right"))
        bucksb = stA.enter_context(nc.sbuf_tensor("bucksb", [128, 64, R], F32, side="right"))
        OB = stA.enter_context(nc.sbuf_tensor("OB", [128, 64, 64], BF16, side="right"))
        scr512 = stA.enter_context(nc.sbuf_tensor("scr512", [128, 512], F32, side="right"))
        crow = stA.enter_context(nc.sbuf_tensor("crow", [1, 4096], BF16, side="right"))
        cmat = stA.enter_context(nc.sbuf_tensor("cmat", [64, 64], BF16, side="right"))
        totr = stA.enter_context(nc.sbuf_tensor("totr", [1, 3, 64], F32, side="right"))
        totq = stA.enter_context(nc.sbuf_tensor("totq", [1, 2, 64], F32, side="right"))
        prefs = stA.enter_context(nc.sbuf_tensor("prefs", [2, 64], BF16, side="right"))
        gmat = stA.enter_context(nc.sbuf_tensor("gmat", [64, 64], F32, side="right"))
        gq = stA.enter_context(nc.sbuf_tensor("gq", [64, 64], F32, side="right"))
        glo = stA.enter_context(nc.sbuf_tensor("glo", [64, 64], F32, side="right"))
        gfl = stA.enter_context(nc.sbuf_tensor("gfl", [2, 4096], BF16, side="right"))
        rk = stA.enter_context(nc.sbuf_tensor("rk", [128, 64], F32, side="right"))
        rhf = stA.enter_context(nc.sbuf_tensor("rhf", [128, 64], F32, side="right"))
        sidx = stA.enter_context(nc.sbuf_tensor("sidx", [128, 64], U32, side="right"))
        pko = stA.enter_context(nc.sbuf_tensor("pko_s", [128, 64], F32, side="right"))

        OBf = OB[:].rearrange("p c b -> p (c b)")

        for u in range(NLU):
            if u > 0:
                p.wait(g, p.tok("gs"))
                p.wait(g, p.tok("yd"))
            lsr = p.dma(g, "gd", SR[:], src_in[u].rearrange("(c p) e -> p c e", p=128))
            lbk = p.dma(g, "gd", bucksb[:], buck_in[u][:, :, :])

            for r in range(R):
                p.wait(v, lbk)
                p.wait(v, p.tok("T"))
                p.wait(v, p.tok("yl"))
                a = p.inc("V", v.tensor_tensor(
                    OB[:, :, :],
                    bucksb[:, :, r].unsqueeze(2).to_broadcast([128, 64, 64]),
                    iob[:].unsqueeze(1).to_broadcast([128, 64, 64]),
                    ALU.is_equal))
                p.wait(t, a)
                p.wait(t, ldc)
                cp = None
                for kb in range(8):
                    p.wait(t, cp)
                    mm = p.inc("T", t.matmul(pS[0:1, :], c1[:, :],
                                             OBf[:, 512 * kb:512 * kb + 512],
                                             start=True, stop=True))
                    p.wait(v, mm)
                    cp = p.inc("V", v.tensor_copy(crow[0:1, 512 * kb:512 * kb + 512],
                                                  pS[0:1, :]))
                p.wait(y, cp)
                p.dma(y, "yl", tmpc.rearrange("j b -> (j b)").unsqueeze(0), crow[0:1, :])
                cm = p.dma(y, "yl", cmat[:, :], tmpc[:, :])
                p.wait(t, cm)
                mm = p.inc("T", t.matmul(pS[0:1, 0:64], c1[0:64, :], cmat[:, :],
                                         start=True, stop=True))
                p.wait(v, mm)
                a = p.inc("V", v.tensor_copy(totr[:, 0, :], pS[0:1, 0:64]))
                p.wait(v, a)
                a = p.inc("V", v.memset(totr[:, 1, :], 0.0))
                p.wait(v, a)
                a = p.inc("V", v.tensor_copy(totr[:, 1, 1:64], totr[:, 0, 0:63]))
                sb_ = 1
                for sh in (1, 2, 4, 8, 16, 32):
                    ob_ = 3 - sb_
                    p.wait(v, a)
                    a = p.inc("V", v.tensor_copy(totr[:, ob_, 0:sh], totr[:, sb_, 0:sh]))
                    p.wait(v, a)
                    a = p.inc("V", v.tensor_tensor(
                        totr[:, ob_, sh:64], totr[:, sb_, sh:64],
                        totr[:, sb_, 0:64 - sh], ALU.add))
                    sb_ = ob_
                p.wait(v, a)
                a = p.inc("V", v.tensor_scalar(totq[:, 0, :], totr[:, sb_, :],
                                               1.0 / 64, 0.4921875,
                                               ALU.mult, ALU.subtract))
                p.wait(v, a)
                a = p.inc("V", v.tensor_scalar(totq[:, 0, :], totq[:, 0, :],
                                               12582912.0, 12582912.0,
                                               ALU.add, ALU.subtract))
                p.wait(v, a)
                a = p.inc("V", v.scalar_tensor_tensor(
                    totq[:, 1, :], totq[:, 0, :], -64.0, totr[:, sb_, :],
                    ALU.mult, ALU.add))
                p.wait(y, a)
                pst = p.dma(y, "yl", tmpp.rearrange("r b -> (r b)").unsqueeze(0),
                            totq[0:1, :, :].rearrange("o r b -> o (r b)"))
                p.wait(g, pst)
                pfd = p.dma(g, "yc", prefs[:, :], tmpp[:, :])
                p.wait(t, pfd)
                mm = p.inc("T", t.matmul(pS[0:64, 0:64], l64[:, :], cmat[:, :],
                                         start=True, stop=False))
                mm = p.inc("T", t.matmul(pS[0:64, 0:64], gfc[0:2, 0:64], prefs[:, :],
                                         start=False, stop=True))
                p.wait(v, mm)
                a = p.inc("V", v.tensor_copy(gmat[:, :], pS[0:64, 0:64]))
                p.wait(v, a)
                a = p.inc("V", v.tensor_scalar(gq[:, :], gmat[:, :],
                                               1.0 / 64, 0.4921875,
                                               ALU.mult, ALU.subtract))
                p.wait(v, a)
                a = p.inc("V", v.tensor_scalar(gq[:, :], gq[:, :],
                                               12582912.0, 12582912.0,
                                               ALU.add, ALU.subtract))
                p.wait(v, a)
                a = p.inc("V", v.scalar_tensor_tensor(
                    glo[:, :], gq[:, :], -64.0, gmat[:, :], ALU.mult, ALU.add))
                p.wait(y, a)
                p.dma(y, "yl", tmpg[0, :, :], gq[:, :])
                gst = p.dma(y, "yl", tmpg[1, :, :], glo[:, :])
                p.wait(g, gst)
                gld = p.dma(g, "yc", gfl[:, :],
                            tmpg.rearrange("r j b -> r (j b)"))
                p.wait(t, gld)
                rtok = {}
                for kb in range(8):
                    bank = P[kb % 2]
                    if kb >= 2:
                        p.wait(t, rtok[kb - 2])
                    mm = p.inc("T", t.matmul(bank[:, :], linc[:, :],
                                             OBf[:, 512 * kb:512 * kb + 512],
                                             start=True, stop=False))
                    mm = p.inc("T", t.matmul(bank[:, :], gfc[:, :],
                                             gfl[:, 512 * kb:512 * kb + 512],
                                             start=False, stop=True))
                    p.wait(v, mm)
                    if kb > 0:
                        p.wait(v, rtok[kb - 1])
                    a = p.inc("V", v.tensor_tensor(
                        scr512[:, :], bank[:, :],
                        OBf[:, 512 * kb:512 * kb + 512], ALU.mult))
                    p.wait(v, a)
                    rtok[kb] = p.inc("V", v.tensor_reduce(
                        rdot[:, 8 * kb:8 * kb + 8],
                        scr512[:].rearrange("p (c b) -> p c b", b=64), AX.X, ALU.add))
                p.wait(v, rtok[6])
                p.wait(v, rtok[7])
                p.wait(v, p.tok("gs"))
                p.wait(v, p.tok("yc"))
                a = p.inc("V", v.tensor_scalar_add(rk[:, :], rdot[:, :], 8192.0 * r - 1.0))
                p.wait(v, a)
                sxd = p.inc("V", v.tensor_copy(sidx[:, :], rk[:, :]))
                p.wait(v, sxd)
                a = p.inc("V", v.tensor_scalar_add(rk[:, :], rdot[:, :], -1.0))
                p.wait(v, a)
                a = p.inc("V", v.tensor_scalar(rhf[:, :], rk[:, :],
                                               1.0 / 64, 0.4921875,
                                               ALU.mult, ALU.subtract))
                p.wait(v, a)
                a = p.inc("V", v.tensor_scalar(rhf[:, :], rhf[:, :],
                                               12582912.0, 12582912.0,
                                               ALU.add, ALU.subtract))
                p.wait(v, a)
                a = p.inc("V", v.scalar_tensor_tensor(
                    rdot[:, :], rhf[:, :], -64.0, rk[:, :], ALU.mult, ALU.add))
                p.wait(v, a)
                a = p.inc("V", v.scalar_tensor_tensor(
                    pko[:, :], rdot[:, :], 128.0, rhf[:, :], ALU.mult, ALU.add))
                p.wait(v, a)
                a = p.inc("V", v.tensor_scalar_add(pko[:, :], pko[:, :], 8192.0 * r))
                p.wait(y, a)
                p.dma(y, "yl", pirk_o[:, u, r, :], pko[:, :])
                p.wait(g, sxd)
                p.wait(g, lsr)
                if u > 0 and r == 0:
                    p.wait(g, p.tok("yd"))
                for c in range(64):
                    p.inc("gs", g.indirect_dma_start(
                        arr[:, :], IndirectOffsetOnAxis(ap=sidx[:, c:c + 1], axis=0),
                        SR[:, c, :], None), 16)

            scat_done = p.tok("gs")
            for r in range(R):
                base = 8192 * r - 64
                p.wait(y, scat_done)
                p.wait(y, p.tok("T"))
                p.wait(y, p.tok("V"))
                p.wait(y, p.tok("S"))
                if r == 0:
                    p.dma(y, "yd", sdb1[:, 0:128], arr[LTOT - 128:LTOT, 0:128], transpose=True)
                    p.dma(y, "yd", sdb1[:, 128:8320], arr[0:8192, 0:128], transpose=True)
                    p.dma(y, "yd", sdb2[:, 0:128], arr[LTOT - 128:LTOT, 128:256], transpose=True)
                    p.dma(y, "yd", sdb2[:, 128:8320], arr[0:8192, 128:256], transpose=True)
                    p.dma(y, "yd", svb[0:64, 0, 0:64], arr[LTOT - 64:LTOT, 0:64])
                    p.dma(y, "yd", svb[0:64, 1, 0:64], arr[0:64, 0:64])
                    rdd = p.dma(y, "yd", svb[0:64, 2:130, 0:64],
                                arr[64:8256, 0:64].rearrange("(j q) e -> q j e", q=64))
                else:
                    p.dma(y, "yd", sdb1[:, :], arr[base - 64:base + 8256, 0:128], transpose=True)
                    p.dma(y, "yd", sdb2[:, :], arr[base - 64:base + 8256, 128:256], transpose=True)
                    rdd = p.dma(y, "yd", svb[0:64, :, 0:64],
                                arr[base:base + 8320, 0:64].rearrange("(j q) e -> q j e", q=64))
                p.wait(t, rdd)
                tk = {}
                for gi in range(32):
                    pD, pP, pB = P[gi % 2], P[2 + gi % 2], P[4 + gi % 2]
                    if gi >= 2:
                        p.wait(t, tk[gi - 2]["stt"])
                        p.wait(t, tk[gi - 2]["eza"])
                        p.wait(t, tk[gi - 2]["epiV"])
                        p.wait(t, tk[gi - 2]["epiS"])
                    c0 = 4 * gi
                    q0 = 64 * c0 + 128
                    t.matmul(pP[0:64, 0:256], auxc[0:2, 0:64], sdb2[0:2, q0:q0 + 256],
                             start=True, stop=False, skip_group_check=True)
                    t.matmul(pP[0:64, 256:512], auxc[0:2, 0:64], sdb2[0:2, q0:q0 + 256],
                             start=False, stop=False, skip_group_check=True)
                    mmp = None
                    for ci in range(4):
                        w = 64 * (c0 + ci) + 64
                        t.matmul(pP[0:64, 64 * ci:64 * ci + 64], sdb2[0:2, w:w + 64],
                                 auxc2[:, :], start=False, stop=False,
                                 skip_group_check=True)
                        mmp = p.inc("T", t.matmul(
                            pP[0:64, 256 + 64 * ci:320 + 64 * ci],
                            sdb2[0:2, w + 64:w + 128], auxc2[:, :],
                            start=False, stop=(ci == 3), skip_group_check=True))
                    mmd = None
                    for ci in range(4):
                        w = 64 * (c0 + ci) + 64
                        qsl = slice(q0 + 64 * ci, q0 + 64 * ci + 64)
                        t.matmul(pD[0:64, 64 * ci:64 * ci + 64],
                                 sdb1[64:128, w:w + 64], sdb2[64:128, qsl],
                                 start=True, stop=True, skip_group_check=True)
                        mmd = p.inc("T", t.matmul(
                            pD[0:64, 256 + 64 * ci:320 + 64 * ci],
                            sdb1[64:128, w + 64:w + 128], sdb2[64:128, qsl],
                            start=True, stop=True, skip_group_check=True))
                    if gi >= 2:
                        p.wait(s, tk[gi - 2]["bo"])
                    p.wait(s, mmd)
                    eza = p.inc("S", s.activation(ez[gi % 2][0:64, :], pD[0:64, :],
                                                  AF.Exp, scale=0.125))
                    p.wait(v, eza)
                    p.wait(v, mmp)
                    stt = p.inc("V", v.scalar_tensor_tensor(
                        ez[gi % 2][0:64, :], pP[0:64, :],
                        0.0, ez[gi % 2][0:64, :], ALU.is_gt, ALU.mult))
                    p.wait(t, stt)
                    for ci in range(4):
                        c = c0 + ci
                        sl = slice(65 * ci, 65 * ci + 65)
                        t.matmul(pB[0:64, sl], ez[gi % 2][0:64, 64 * ci:64 * ci + 64],
                                 svb[0:64, c, :], start=True, stop=False,
                                 skip_group_check=True)
                        t.matmul(pB[0:64, sl],
                                 ez[gi % 2][0:64, 256 + 64 * ci:320 + 64 * ci],
                                 svb[0:64, c + 1, :], start=False, stop=False,
                                 skip_group_check=True)
                        bo = p.inc("T", t.matmul(
                            pB[0:64, sl], epsid[0:64, :], svb[0:64, c + 1, :],
                            start=False, stop=True, skip_group_check=True))
                    if gi == 0 or gi == 16:
                        p.wait(v, p.tok("yd"))
                        p.wait(s, p.tok("yd"))
                    p.wait(v, bo)
                    pBv = pB[0:64, 0:260].rearrange("q (c e) -> q c e", e=65)
                    col0 = c0 if gi < 16 else c0 - 64
                    ctok = p.inc("V", v.tensor_copy(rsq2[:, :], pBv[:, :, 64]))
                    p.wait(v, ctok)
                    a = p.inc("V", v.reciprocal(rsq[:, :], rsq2[:, :]))
                    p.wait(s, ctok)
                    epiS = p.inc("S", s.activation(stg[:, col0:col0 + 4, 64],
                                                   rsq2[:, :], AF.Ln))
                    p.wait(v, a)
                    epiV = p.inc("V", v.tensor_tensor(
                        stg[:, col0:col0 + 4, 0:64], pBv[:, :, 0:64],
                        rsq[:, :].unsqueeze(2).to_broadcast([64, 4, 64]), ALU.mult))
                    tk[gi] = {"eza": eza, "stt": stt, "bo": bo,
                              "epiV": epiV, "epiS": epiS}
                    if gi == 15 or gi == 31:
                        p.wait(y, epiV)
                        p.wait(y, epiS)
                        half = 0 if gi == 15 else 64
                        p.dma(y, "yd",
                              oarr[u][8192 * r:8192 * r + 8192, :]
                              .rearrange("(i c) e -> i c e", c=128)[:, half:half + 64, :],
                              stg[:, :, :])

        p.wait(g, p.tok("gs"))
        p.wait(g, p.tok("yd"))

    with ExitStack() as stB:
        LQ = stB.enter_context(nc.sbuf_tensor("LQ", [DH, T], BF16, side="right"))
        LK = stB.enter_context(nc.sbuf_tensor("LK", [DH, T], BF16, side="right"))
        LV = stB.enter_context(nc.sbuf_tensor("LV", [128, 64, DH + 1], BF16, side="right"))
        ezL = stB.enter_context(nc.sbuf_tensor("ezL", [128, 2, 2, 128], BF16, side="right"))
        locst = stB.enter_context(nc.sbuf_tensor("locst", [128, 64, DH], BF16, side="right"))
        rsL = stB.enter_context(nc.sbuf_tensor("rsL", [128, 1], F32, side="right"))
        p.dma(g, "gd", LQ[:], locq_in[:, :])
        p.dma(g, "gd", LK[:], lock_in[:, :])
        lvd = p.dma(g, "gd", LV[:], locv_in[:, :, :])
        p.wait(t, lvd)
        ltk = {}
        for i in range(64):
            pD, pB = P[i % 2], P[4 + i % 2]
            if i >= 2:
                p.wait(t, ltk[i - 2]["ez"])
                p.wait(t, ltk[i - 2]["epi"])
            if i > 0:
                t.matmul(pD[:, 0:128], LK[:, 128 * (i - 1):128 * i],
                         LQ[:, 128 * i:128 * i + 128], start=True, stop=False,
                         skip_group_check=True)
            mmB = p.inc("T", t.matmul(pD[:, 128:256], LK[:, 128 * i:128 * i + 128],
                                      LQ[:, 128 * i:128 * i + 128],
                                      start=(i == 0), stop=True,
                                      skip_group_check=True))
            if i >= 2:
                p.wait(s, ltk[i - 2]["bo"])
            p.wait(s, mmB)
            if i > 0:
                p.inc("S", s.activation(ezL[:, i % 2, 0, :], pD[:, 0:128],
                                        AF.Exp, scale=0.125))
            ezt = p.inc("S", s.activation(ezL[:, i % 2, 1, :], pD[:, 128:256],
                                          AF.Exp, scale=0.125))
            p.wait(v, ezt)
            msk = p.inc("V", v.tensor_tensor(ezL[:, i % 2, 1, :], ezL[:, i % 2, 1, :],
                                             trim[:, :], ALU.mult))
            p.wait(t, msk)
            if i > 0:
                t.matmul(pB[:, 0:65], ezL[:, i % 2, 0, :], LV[:, i - 1, :],
                         start=True, stop=False, skip_group_check=True)
            t.matmul(pB[:, 0:65], ezL[:, i % 2, 1, :], LV[:, i, :],
                     start=(i == 0), stop=False, skip_group_check=True)
            bo = p.inc("T", t.matmul(pB[:, 0:65], epsi128[:, :], LV[:, i, :],
                                     start=False, stop=True, skip_group_check=True))
            p.wait(v, bo)
            a = p.inc("V", v.reciprocal(rsL[:, :], pB[:, 64:65]))
            p.wait(v, a)
            epi = p.inc("V", v.tensor_tensor(
                locst[:, i, :], pB[:, 0:64],
                rsL[:, :].to_broadcast([128, 64]), ALU.mult))
            ltk[i] = {"ez": ezt, "bo": bo, "epi": epi}
        p.wait(y, p.tok("V"))
        p.dma(y, "yd", loco_o[:, :, :], locst[:])
        p.wait(y, p.tok("yd"))
        p.wait(y, p.tok("yl"))
        p.wait(y, p.tok("yc"))

    st.close()
    return nc


def l2_consts():
    eps = EPSSELF
    epsid = np.zeros((128, 64), np.float32)
    epsid[0:64] = eps * np.eye(64, dtype=np.float32)
    epsid[64:128] = eps * np.eye(64, dtype=np.float32)
    trim = (np.arange(128)[:, None] < np.arange(128)[None, :]).astype(np.float32)
    return {
        "ones_col": np.ones((128, 1), np.float32),
        "iotaB": np.broadcast_to(np.arange(64, dtype=np.float32), (128, 64)).copy(),
        "lincT": (np.arange(128)[:, None] <= np.arange(128)[None, :]).astype(np.float32),
        "l64sT": (np.arange(64)[:, None] < np.arange(64)[None, :]).astype(np.float32),
        "gfc": np.repeat(np.array([[64.0], [1.0]], np.float32), 128, axis=1),
        "auxc": np.repeat(np.array([[256.0], [1.0]], np.float32), 128, axis=1),
        "auxc2": np.repeat(np.array([[-256.0], [-1.0]], np.float32), 64, axis=1),
        "epsid": epsid,
        "epsi128": (eps * np.eye(128)).astype(np.float32),
        "trimask": trim,
    }


def l2_src(qk_rows, kh_rows, v_rows):
    out = np.zeros((T, 256), BF)
    out[:, 0:64] = v_rows
    out[:, 64:128] = kh_rows
    tt = np.arange(T)
    out[:, 128] = (tt // 256).astype(BF)
    out[:, 129] = (tt % 256).astype(BF)
    out[:, 192:256] = qk_rows
    return out



def emit_lnT(p, nc, src, dst, nch, ntok, c1, r1, sq, stats, mrow, rrow, ps_s, ps_b=None,
             gamma=None, beta=None):
    if ps_b is None:
        ps_b = ps_s
    """dst = LN(src) (optionally *gamma+beta per feature) in feature-major bf16.

    src/dst: sbuf [128, nch, ntok] bf16 (may alias); c1: ones col (128,1) bf16;
    r1: ones row (1,128) bf16; sq: scratch [128,2,512] bf16; stats [1,3,ntok] f32;
    mrow/rrow [1,ntok] bf16; ps_s: two psum banks for stats; ps_b: two psum banks
    for broadcast; gamma/beta: sbuf [128, nch] f32 per-feature or None.
    """
    t, v, s = nc.tensor, nc.vector, nc.scalar
    ntile = ntok // 512
    E_ = 128 * nch
    mm_sq = {}
    mmt = None
    for nt in range(ntile):
        sl = slice(512 * nt, 512 * nt + 512)
        for c in range(nch):
            sb = c % 2
            k = nt * nch + c
            if k >= 2:
                p.wait(v, mm_sq[k - 2])
            sqd = p.inc("V", v.tensor_tensor(sq[:, sb, :], src[:, c, sl], src[:, c, sl], ALU.mult))
            t.matmul(ps_s[0][0:1, :], c1[:, :], src[:, c, sl], start=(c == 0), stop=(c == nch - 1))
            p.wait(t, sqd)
            mm_sq[k] = mmt = p.inc(
                "T", t.matmul(ps_s[1][0:1, :], c1[:, :], sq[:, sb, :],
                              start=(c == 0), stop=(c == nch - 1)))
        p.wait(s, mmt)
        p.inc("S", s.copy(stats[:, 0, sl], ps_s[0][0:1, :]))
        ev = p.inc("S", s.copy(stats[:, 1, sl], ps_s[1][0:1, :]))
        p.wait(t, ev)
    p.wait(v, p.tok("S"))
    a = p.inc("V", v.tensor_scalar_mul(stats[:, 0, :], stats[:, 0, :], 1.0 / E_))
    p.wait(v, a)
    a = p.inc("V", v.tensor_tensor(stats[:, 2, :], stats[:, 0, :], stats[:, 0, :], ALU.mult))
    p.wait(v, a)
    a = p.inc("V", v.scalar_tensor_tensor(stats[:, 1, :], stats[:, 1, :], 1.0 / E_,
                                          stats[:, 2, :], ALU.mult, ALU.subtract))
    p.wait(v, a)
    a = p.inc("V", v.tensor_scalar_add(stats[:, 1, :], stats[:, 1, :], EPS))
    p.wait(s, a)
    a = p.inc("S", s.activation(stats[:, 2, :], stats[:, 1, :], AF.Sqrt))
    p.wait(v, a)
    a = p.inc("V", v.reciprocal(stats[:, 1, :], stats[:, 2, :]))
    p.wait(v, a)
    a = p.inc("V", v.tensor_copy(mrow[:, :ntok], stats[:, 0, :]))
    p.wait(v, a)
    sd = p.inc("V", v.tensor_copy(rrow[:, :ntok], stats[:, 1, :]))
    p.wait(t, sd)
    for nt in range(ntile):
        sl = slice(512 * nt, 512 * nt + 512)
        p.inc("T", t.matmul(ps_b[0][:, :], r1[:, :], mrow[:, sl], start=True, stop=True))
        bc2 = p.inc("T", t.matmul(ps_b[1][:, :], r1[:, :], rrow[:, sl], start=True, stop=True))
        p.wait(v, bc2)
        dn = None
        for c in range(nch):
            a = p.inc("V", v.tensor_tensor(dst[:, c, sl], src[:, c, sl], ps_b[0][:, :], ALU.subtract))
            p.wait(v, a)
            if gamma is None:
                dn = p.inc("V", v.tensor_tensor(dst[:, c, sl], dst[:, c, sl], ps_b[1][:, :], ALU.mult))
            else:
                a = p.inc("V", v.tensor_tensor(dst[:, c, sl], dst[:, c, sl], ps_b[1][:, :], ALU.mult))
                p.wait(v, a)
                dn = p.inc("V", v.scalar_tensor_tensor(
                    dst[:, c, sl], dst[:, c, sl], gamma[:, c:c + 1],
                    beta[:, c:c + 1].to_broadcast([128, 512]), ALU.mult, ALU.add))
        p.wait(t, dn)
    return p.tok("V")


def emit_proj(p, nc, lhs_sb, nfi, rhs, dst, nfo, ntok, ps, bank0, nbank,
              bias=None, act=None, accum_into=None, evac="S"):
    t, v, s = nc.tensor, nc.vector, nc.scalar
    ntile = ntok // 512
    for fo in range(nfo):
        mm = None
        for fi in range(nfi):
            lhs = lhs_sb[:, fi, 128 * fo:128 * fo + 128]
            for nt in range(ntile):
                mm = t.matmul(ps[bank0 + nt % nbank][:, :], lhs,
                              rhs[:, fi, 512 * nt:512 * nt + 512],
                              start=(fi == 0), stop=(fi == nfi - 1))
        mmt = p.inc("T", mm)
        ev = None
        for nt in range(ntile):
            pb = ps[bank0 + nt % nbank][:, :]
            dsl = dst[:, fo, 512 * nt:512 * nt + 512]
            if act is not None:
                p.wait(s, mmt)
                if bias is not None:
                    ev = p.inc("S", s.activation(dsl, pb, act, bias=bias[:, fo:fo + 1]))
                else:
                    ev = p.inc("S", s.activation(dsl, pb, act))
            elif bias is not None:
                p.wait(v, mmt)
                ev = p.inc("V", v.tensor_scalar_add(dsl, pb, bias[:, fo:fo + 1]))
            else:
                p.wait(s, mmt)
                ev = p.inc("S", s.copy(dsl, pb))
        p.wait(t, ev)
    return ev


def emit_transpose_chunks(p, nc, srcT, nch, ntok, dst_tok, ps, bank0, nbank, ident):
    t, v, s = nc.tensor, nc.vector, nc.scalar
    ntile = ntok // 128
    ev = None
    for tt_ in range(ntile):
        for c in range(nch):
            bk = (tt_ * nch + c) % nbank
            mm = p.inc("T", t.transpose(ps[bank0 + bk][:, 0:128],
                                        srcT[:, c, 128 * tt_:128 * tt_ + 128], ident[:, :]))
            p.wait(v, mm)
            ev = p.inc("V", v.tensor_copy(dst_tok[:, tt_, 128 * c:128 * c + 128],
                                          ps[bank0 + bk][:, 0:128]))
            p.wait(t, ev)
    return ev




def build_L3():
    nc = bass.Bass("TRN2", target_bir_lowering=False)

    def I(name, shape, dt=F32):
        return nc.dram_tensor(name, shape, dt, kind="ExternalInput").ap()

    xT = I("xT", [E, TOKS])
    xrows = I("xrows", [TOKS, E])
    locT = I("locT", [NLOC * DH, TOKS], BF16)
    oarr = [I(f"oarr{u}", [R * T + 64, 66]) for u in range(NLSH)]
    pirk = I("pirk", [128, NLSH, R, TOKS // 128])
    woT = I("woT", [E, E])
    b_out = I("b_outF", [128, 8])
    f1T = I("f1T", [E, FF])
    b_ff1 = I("b_ff1F", [128, 32])
    f2T = I("f2T", [FF, E])
    b_ff2 = I("b_ff2F", [128, 8])
    g1v = I("ln1gF", [128, 8]); be1 = I("ln1bF", [128, 8])
    g2v = I("ln2gF", [128, 8]); be2 = I("ln2bF", [128, 8])
    wg = I("Wg", [E, EX])
    wup = I("Wup", [EX, E, HH])
    bupf = I("bupF", [EX, 128, 32])
    wdn = I("Wdown", [EX, HH, E])
    bdnf = I("bdownF", [EX, 128, 8])
    ones_col = I("ones_col", [128, 1])
    expsel = I("expsel", [EX, EX, 128])
    ones_row = I("ones_row", [1, 128])
    identity = I("identity", [128, 128])
    x2o = nc.dram_tensor("x2", [TOKS, E], F32, kind="ExternalOutput").ap()
    x1scr = nc.dram_tensor("x1scr", [TOKS, E], F32, kind="Internal").ap()

    st = ExitStack()
    p = Prog(nc, st, ["T", "V", "S", "gd", "gl", "gw0", "gw1", "gx", "gi0", "gi1", "yd"])
    t, v, s, g, y = nc.tensor, nc.vector, nc.scalar, nc.gpsimd, nc.sync
    NCH = 8
    NTT = TOKS // 128

    c1 = st.enter_context(nc.sbuf_tensor("c1", [128, 1], BF16))
    r1 = st.enter_context(nc.sbuf_tensor("r1", [1, 128], BF16))
    idn = st.enter_context(nc.sbuf_tensor("idn", [128, 128], BF16))
    idnf = st.enter_context(nc.sbuf_tensor("idnf", [128, 128], F32))
    sq = st.enter_context(nc.sbuf_tensor("sq", [128, 2, 512], BF16))
    stats = st.enter_context(nc.sbuf_tensor("stats", [1, 3, TOKS], F32))
    mrow = st.enter_context(nc.sbuf_tensor("mrow", [1, TOKS], BF16))
    rrow = st.enter_context(nc.sbuf_tensor("rrow", [1, TOKS], BF16))
    bo_c = st.enter_context(nc.sbuf_tensor("bo_c", [128, 8], F32))
    b2_c = st.enter_context(nc.sbuf_tensor("b2_c", [128, 8], F32))
    b1_c = st.enter_context(nc.sbuf_tensor("b1_c", [128, 32], F32))
    lnv = st.enter_context(nc.sbuf_tensor("lnv", [128, 4, 8], F32))
    bm = st.enter_context(nc.sbuf_tensor("bm", [128, 40], F32))
    pk32 = st.enter_context(nc.sbuf_tensor("pk32", [128, NLSH, R, NTT], U32))
    wgs = st.enter_context(nc.sbuf_tensor("wgs", [128, 8, 16], BF16))
    esel = st.enter_context(nc.sbuf_tensor("esel", [EX, EX, 128], BF16))
    glT = st.enter_context(nc.sbuf_tensor("glT", [16, TOKS], F32))
    cwk = st.enter_context(nc.sbuf_tensor("cwk", [128, NTT, 12], F32))
    oh3 = st.enter_context(nc.sbuf_tensor("oh3", [128, 3, NTT, 12], BF16))
    w3 = st.enter_context(nc.sbuf_tensor("w3", [128, 8, NTT], F32))
    cw = st.enter_context(nc.sbuf_tensor("cw", [128, NTT, 12], F32))
    cwb = st.enter_context(nc.sbuf_tensor("cwb", [128, NTT, 12], BF16))
    cwT = st.enter_context(nc.sbuf_tensor("cwT", [16, TOKS], BF16))
    xt128 = st.enter_context(nc.sbuf_tensor("xt128", [128, E], F32))
    Bb = st.enter_context(nc.sbuf_tensor("bufB", [128, NCH, TOKS], BF16))
    C = st.enter_context(nc.sbuf_tensor("bufC", [128, NCH, TOKS], BF16))
    WB = [st.enter_context(nc.sbuf_tensor(f"wb{i}", [128, 32, 128], BF16))
          for i in range(2)]
    ps = [st.enter_context(nc.psum_tensor(f"ps{i}", [128, 512], F32)) for i in range(6)]
    psb = [st.enter_context(nc.psum_tensor(f"psb{i}", [128, 512], BF16)) for i in range(2)]
    stR = st.enter_context(ExitStack())
    A = stR.enter_context(nc.sbuf_tensor("bufA1", [128, NCH, TOKS], BF16, side="right"))

    p.dma(g, "gd", A[:], xT.rearrange("(c p) n -> p c n", p=128))
    p.dma(g, "gd", c1[:], ones_col[:, :])
    p.dma(g, "gd", r1[:], ones_row[:, :])
    p.dma(g, "gd", idn[:], identity[:, :])
    p.dma(g, "gd", idnf[:], identity[:, :])
    p.dma(g, "gd", bo_c[:, :], b_out[:, :])
    p.dma(g, "gd", b2_c[:, :], b_ff2[:, :])
    p.dma(g, "gd", b1_c[:, :], b_ff1[:, :])
    p.dma(g, "gd", lnv[:, 0, :], g1v[:, :])
    p.dma(g, "gd", lnv[:, 1, :], be1[:, :])
    p.dma(g, "gd", lnv[:, 2, :], g2v[:, :])
    p.dma(g, "gd", lnv[:, 3, :], be2[:, :])
    p.dma(g, "gd", wgs[:, :, 0:EX], wg.rearrange("(c p) n -> p c n", p=128))
    ld0 = p.dma(g, "gd", esel[:], expsel[:, :, :])

    with ExitStack() as st0:
        pkf = st0.enter_context(
            nc.sbuf_tensor("pkf", [128, NLSH // 2, R, NTT], F32, side="right"))
        for ph in range(2):
            hs = slice(6 * ph, 6 * ph + 6)
            pkld = p.dma(g, "gd", pkf[:], pirk[:, hs, :, :])
            p.wait(v, pkld)
            cast_done = p.inc("V", v.tensor_copy(pk32[:, hs, :, :], pkf[:]))
            p.wait(g, cast_done)

    p.wait(t, ld0)
    p.wait(v, ld0)
    emit_lnT(p, nc, A, Bb, NCH, TOKS, c1, r1, sq, stats, mrow, rrow,
             (ps[4], ps[5]))
    gt_done = p.tok("V")
    stR.close()
    stD = st.enter_context(ExitStack())
    D = stD.enter_context(nc.sbuf_tensor("bufD", [128, NCH, TOKS], BF16, side="right"))
    p.wait(g, gt_done)
    p.dma(g, "gl", D[:, 0, :], locT[0:128, :])
    locld = p.dma(g, "gl", D[:, 1, :], locT[128:256, :])

    with ExitStack() as st2:
        og = st2.enter_context(nc.sbuf_tensor("og", [128, R, NTT, 66], F32, side="right"))
        lw = st2.enter_context(nc.sbuf_tensor("lw", [128, 6, NTT], F32, side="right"))
        wo = st2.enter_context(nc.sbuf_tensor("wo", [128, NTT, 2, 64], BF16, side="right"))
        wo2 = st2.enter_context(nc.sbuf_tensor("wo2", [128, NTT, 64], F32, side="right"))
        p.wait(g, cast_done)
        wo_free = None
        for u in range(NLSH):
            gt = None
            gsem = "gi0" if u % 2 == 0 else "gi1"
            for r in range(R):
                for c in range(NTT):
                    gt = p.inc(gsem, g.indirect_dma_start(
                        og[:, r, c, :], None, oarr[u][:, :],
                        IndirectOffsetOnAxis(ap=pk32[:, u, r, c:c + 1], axis=0)), 16)
            p.wait(v, gt)
            lse = og[:, :, :, 64]
            a = p.inc("V", v.tensor_tensor(lw[:, 4, :], lse[:, 0, :], lse[:, 1, :], ALU.max))
            p.wait(v, a)
            a = p.inc("V", v.tensor_tensor(lw[:, 4, :], lw[:, 4, :], lse[:, 2, :], ALU.max))
            p.wait(v, a)
            a = p.inc("V", v.tensor_tensor(lw[:, 4, :], lw[:, 4, :], lse[:, 3, :], ALU.max))
            p.wait(v, a)
            a = p.inc("V", v.tensor_tensor(
                lw[:, 0:4, :], lse[:, :, :],
                lw[:, 4:5, :].to_broadcast([128, 4, NTT]), ALU.subtract))
            p.wait(v, a)
            a = p.inc("V", v.tensor_scalar_max(lw[:, 0:4, :], lw[:, 0:4, :], -80.0))
            p.wait(s, a)
            a = p.inc("S", s.activation(lw[:, 0:4, :], lw[:, 0:4, :], AF.Exp))
            p.wait(v, a)
            a = p.inc("V", v.tensor_reduce(
                lw[:, 5, :], lw[:, 0:4, :].transpose([0, 2, 1]), AX.X, ALU.add))
            p.wait(v, a)
            a = p.inc("V", v.reciprocal(lw[:, 5, :], lw[:, 5, :]))
            p.wait(v, a)
            a = p.inc("V", v.tensor_tensor(
                lw[:, 0:4, :], lw[:, 0:4, :],
                lw[:, 5:6, :].to_broadcast([128, 4, NTT]), ALU.mult))
            p.wait(v, a)
            half = u % 2
            dn = None
            for r in range(R):
                wbr = lw[:, r, :].unsqueeze(2).to_broadcast([128, NTT, 64])
                a = p.inc("V", v.tensor_tensor(wo2[:, :, :], og[:, r, :, 0:64], wbr, ALU.mult))
                p.wait(v, a)
                if r == 0:
                    if half == 0 and wo_free is not None:
                        p.wait(v, wo_free)
                    dn = p.inc("V", v.tensor_copy(wo[:, :, half, :], wo2[:, :, :]))
                else:
                    dn = p.inc("V", v.tensor_tensor(wo[:, :, half, :], wo[:, :, half, :],
                                                    wo2[:, :, :], ALU.add))
                p.wait(v, dn)
            p.wait(g, dn)
            if half == 1:
                cc = 2 + u // 2
                p.wait(t, dn)
                ev = None
                for tt_ in range(NTT):
                    bk = tt_ % 2
                    mm = p.inc("T", t.transpose(
                        psb[bk][:, 0:128],
                        wo[:, tt_, :, :].rearrange("p a b -> p (a b)"), idn[:, :]))
                    p.wait(v, mm)
                    ev = p.inc("V", v.tensor_copy(D[:, cc, 128 * tt_:128 * tt_ + 128],
                                                  psb[bk][:, 0:128]))
                    p.wait(t, ev)
                wo_free = p.tok("T")
        out_done = p.tok("V")

    def stream_w(dramT, nfi, fo, wb):
        return p.dma(g, "gw0" if fo % 2 == 0 else "gw1",
            wb[:, 0:nfi, :],
            dramT.rearrange("(c p) n -> p c n", p=128)[:, :, 128 * fo:128 * fo + 128])

    p.wait(t, out_done)
    p.wait(t, locld)
    p.wait(g, p.tok("T"))
    wt = {0: stream_w(woT, 8, 0, WB[0])}
    prev_mmt = None
    for fo in range(8):
        if fo + 1 < 8:
            if prev_mmt is not None:
                p.wait(g, prev_mmt)
            wt[fo + 1] = stream_w(woT, 8, fo + 1, WB[(fo + 1) % 2])
        p.wait(t, wt[fo])
        mm = None
        for fi in range(8):
            lhs = WB[fo % 2][:, fi, :]
            for nt in range(NT):
                mm = t.matmul(ps[nt][:, :], lhs, D[:, fi, 512 * nt:512 * nt + 512],
                              start=(fi == 0), stop=(fi == 7))
        mmt = p.inc("T", mm)
        prev_mmt = mmt
        p.wait(v, mmt)
        ev = None
        for nt in range(NT):
            sl = slice(512 * nt, 512 * nt + 512)
            ev = p.inc("V", v.scalar_tensor_tensor(
                C[:, fo, sl], ps[nt][:, :], bo_c[:, fo:fo + 1], Bb[:, fo, sl],
                ALU.add, ALU.add))
        p.wait(t, ev)
    r1_done = p.tok("V")

    p.wait(t, r1_done)
    p.wait(v, r1_done)
    emit_lnT(p, nc, C, C, NCH, TOKS, c1, r1, sq, stats, mrow, rrow,
             (ps[4], ps[5]),
             gamma=lnv[:, 0, :], beta=lnv[:, 1, :])

    with ExitStack() as st3:
        EH = st3.enter_context(nc.sbuf_tensor("bufE", [128, 32, 512], BF16, side="right"))
        for nt in range(NT):
            sl = slice(512 * nt, 512 * nt + 512)
            p.wait(g, p.tok("T"))
            wt = {0: stream_w(f1T, 8, 0, WB[0])}
            prev_mmt = None
            for fo in range(32):
                if fo + 1 < 32:
                    if prev_mmt is not None:
                        p.wait(g, prev_mmt)
                    wt[fo + 1] = stream_w(f1T, 8, fo + 1, WB[(fo + 1) % 2])
                p.wait(t, wt[fo])
                mm = None
                for fi in range(8):
                    mm = t.matmul(ps[fo % 4][:, :], WB[fo % 2][:, fi, :], C[:, fi, sl],
                                  start=(fi == 0), stop=(fi == 7))
                mmt = p.inc("T", mm)
                prev_mmt = mmt
                p.wait(s, mmt)
                ev = p.inc("S", s.activation(EH[:, fo, :], ps[fo % 4][:, :], AF.Gelu,
                                             bias=b1_c[:, fo:fo + 1]))
                p.wait(t, ev)
            p.wait(g, p.tok("T"))
            wt = {0: stream_w(f2T, 32, 0, WB[0])}
            prev_mmt = None
            for fo in range(8):
                if fo + 1 < 8:
                    if prev_mmt is not None:
                        p.wait(g, prev_mmt)
                    wt[fo + 1] = stream_w(f2T, 32, fo + 1, WB[(fo + 1) % 2])
                p.wait(t, wt[fo])
                mm = None
                for fi in range(32):
                    mm = t.matmul(ps[4 + fo % 2][:, :], WB[fo % 2][:, fi, :], EH[:, fi, :],
                                  start=(fi == 0), stop=(fi == 31))
                mmt = p.inc("T", mm)
                prev_mmt = mmt
                p.wait(v, mmt)
                ev = p.inc("V", v.scalar_tensor_tensor(
                    D[:, fo, sl], ps[4 + fo % 2][:, :], b2_c[:, fo:fo + 1],
                    C[:, fo, sl], ALU.add, ALU.add))
                p.wait(t, ev)
    r2_done = p.tok("V")

    p.wait(t, r2_done)
    p.wait(v, r2_done)
    emit_lnT(p, nc, D, D, NCH, TOKS, c1, r1, sq, stats, mrow, rrow,
             (ps[4], ps[5]),
             gamma=lnv[:, 2, :], beta=lnv[:, 3, :])
    bo_done = p.tok("V")
    stA2 = st.enter_context(ExitStack())
    A = stA2.enter_context(nc.sbuf_tensor("bufA2", [128, NCH, TOKS], BF16, side="right"))
    p.wait(g, bo_done)
    ldx = p.dma(g, "gd", A[:], xT.rearrange("(c p) n -> p c n", p=128))
    p.wait(v, bo_done)
    p.wait(v, ldx)
    x1t_done = None
    for c in range(NCH):
        x1t_done = p.inc("V", v.tensor_tensor(A[:, c, :], A[:, c, :], D[:, c, :], ALU.add))

    p.wait(t, bo_done)
    for tt_ in range(NTT):
        lx = p.dma(g, "gx", xt128[:, :], xrows[128 * tt_:128 * tt_ + 128, :])
        ev = None
        for c in range(NCH):
            bk = c % 2
            mm = p.inc("T", t.transpose(psb[bk][:, 0:128],
                                        D[:, c, 128 * tt_:128 * tt_ + 128], idn[:, :]))
            p.wait(v, mm)
            if c == 0:
                p.wait(v, lx)
            ev = p.inc("V", v.tensor_tensor(xt128[:, 128 * c:128 * c + 128],
                                            xt128[:, 128 * c:128 * c + 128],
                                            psb[bk][:, 0:128], ALU.add))
            p.wait(t, ev)
        p.wait(y, ev)
        sv = p.dma(y, "yd", x1scr[128 * tt_:128 * tt_ + 128, :], xt128[:, :])
        p.wait(g, sv)

    p.wait(t, x1t_done)
    p.wait(v, x1t_done)
    emit_lnT(p, nc, A, Bb, NCH, TOKS, c1, r1, sq, stats, mrow, rrow,
             (ps[4], ps[5]))
    g2_done = p.tok("V")

    p.wait(t, g2_done)
    for nt in range(NT):
        sl = slice(512 * nt, 512 * nt + 512)
        mm = None
        for fi in range(8):
            mm = t.matmul(ps[4][0:EX, :], wgs[:, fi, 0:EX], Bb[:, fi, sl],
                          start=(fi == 0), stop=(fi == 7))
        mmt = p.inc("T", mm)
        p.wait(s, mmt)
        ev = p.inc("S", s.copy(glT[0:EX, sl], ps[4][0:EX, :]))
        p.wait(t, ev)
    gl_done = p.tok("S")
    p.wait(t, gl_done)
    for tt_ in range(NTT):
        bk = tt_ % 2
        mm = p.inc("T", t.transpose(ps[2 + bk][0:128, 0:16],
                                    glT[0:16, 128 * tt_:128 * tt_ + 128], idnf[0:16, 0:16]))
        p.wait(v, mm)
        ev = p.inc("V", v.tensor_copy(cwk[:, tt_, 0:EX], ps[2 + bk][:, 0:EX]))
        p.wait(t, ev)
    p.wait(v, p.tok("V"))
    for k in range(3):
        a = p.inc("V", v.tensor_reduce(w3[:, k, :], cwk[:, :, :], AX.X, ALU.max))
        p.wait(v, a)
        a = p.inc("V", v.tensor_tensor(
            oh3[:, k, :, :], cwk[:, :, :],
            w3[:, k, :].unsqueeze(2).to_broadcast([128, NTT, EX]), ALU.is_ge))
        p.wait(v, a)
        if k < 2:
            a = p.inc("V", v.scalar_tensor_tensor(
                cwk[:, :, :], oh3[:, k, :, :], NEGBIG, cwk[:, :, :],
                ALU.mult, ALU.add))
            p.wait(v, a)
    a = p.inc("V", v.tensor_tensor(w3[:, 3, :], w3[:, 1, :], w3[:, 0, :], ALU.subtract))
    p.wait(v, a)
    a = p.inc("V", v.tensor_tensor(w3[:, 4, :], w3[:, 2, :], w3[:, 0, :], ALU.subtract))
    p.wait(s, a)
    a = p.inc("S", s.activation(w3[:, 3:5, :], w3[:, 3:5, :], AF.Exp))
    p.wait(v, a)
    a = p.inc("V", v.tensor_tensor(w3[:, 5, :], w3[:, 3, :], w3[:, 4, :], ALU.add))
    p.wait(v, a)
    a = p.inc("V", v.tensor_scalar_add(w3[:, 5, :], w3[:, 5, :], 1.0))
    p.wait(v, a)
    a = p.inc("V", v.reciprocal(w3[:, 5, :], w3[:, 5, :]))
    p.wait(v, a)
    a = p.inc("V", v.tensor_copy(w3[:, 0, :], w3[:, 5, :]))
    p.wait(v, a)
    a = p.inc("V", v.tensor_tensor(w3[:, 1, :], w3[:, 3, :], w3[:, 5, :], ALU.mult))
    p.wait(v, a)
    a = p.inc("V", v.tensor_tensor(w3[:, 2, :], w3[:, 4, :], w3[:, 5, :], ALU.mult))
    p.wait(v, a)
    a = p.inc("V", v.tensor_tensor(
        oh3[:, :, :, :], oh3[:, :, :, :],
        w3[:, 0:3, :].unsqueeze(3).to_broadcast([128, 3, NTT, EX]), ALU.mult))
    p.wait(v, a)
    a = p.inc("V", v.tensor_reduce(cw[:, :, :], oh3[:].transpose([0, 2, 3, 1]), AX.X, ALU.add))
    p.wait(v, a)
    a = p.inc("V", v.tensor_copy(cwb[:, :, :], cw[:, :, :]))
    p.wait(t, a)
    cw_done = None
    for tt_ in range(NTT):
        bk = tt_ % 2
        mm = p.inc("T", t.transpose(psb[bk][0:EX, 0:128],
                                    cwb[:, tt_, :], idn[:, :]))
        p.wait(v, mm)
        cw_done = p.inc("V", v.tensor_copy(cwT[0:EX, 128 * tt_:128 * tt_ + 128],
                                           psb[bk][0:EX, 0:128]))
        p.wait(t, cw_done)

    mz = None
    for c in range(NCH):
        mz = p.inc("V", v.memset(C[:, c, :], 0.0))
    p.wait(v, mz)
    stA2.close()
    stD.close()
    with ExitStack() as st4:
        EH2 = st4.enter_context(nc.sbuf_tensor("bufE2", [128, 32, 1024], BF16, side="right"))
        scr5 = st4.enter_context(nc.sbuf_tensor("scr5", [128, 2, 512], BF16, side="right"))
        p.wait(g, p.tok("V"))
        p.wait(g, p.tok("T"))
        for e in range(EX):
            p.wait(g, p.tok("V"))
            p.wait(g, p.tok("S"))
            lb = p.dma(g, "gl", bm[:, 0:32], bupf[e, :, :])
            lb = p.dma(g, "gl", bm[:, 32:40], bdnf[e, :, :])
            for hf in range(2):
                hsl = slice(1024 * hf, 1024 * hf + 1024)
                p.wait(g, p.tok("T"))
                wt = {0: stream_w(wup[e, :, :], 8, 0, WB[0])}
                prev_mmt = None
                for fo in range(32):
                    if fo + 1 < 32:
                        if prev_mmt is not None:
                            p.wait(g, prev_mmt)
                        wt[fo + 1] = stream_w(wup[e, :, :], 8, fo + 1, WB[(fo + 1) % 2])
                    p.wait(t, wt[fo])
                    if fo == 0:
                        p.wait(s, lb)
                    mm = None
                    for fi in range(8):
                        for n2 in range(2):
                            mm = t.matmul(ps[2 * (fo % 2) + n2][:, :], WB[fo % 2][:, fi, :],
                                          Bb[:, fi, 1024 * hf + 512 * n2:1024 * hf + 512 * n2 + 512],
                                          start=(fi == 0), stop=(fi == 7))
                    mmt = p.inc("T", mm)
                    prev_mmt = mmt
                    p.wait(s, mmt)
                    ev = None
                    for n2 in range(2):
                        ev = p.inc("S", s.activation(
                            EH2[:, fo, 512 * n2:512 * n2 + 512],
                            ps[2 * (fo % 2) + n2][:, :], AF.Gelu, bias=bm[:, fo:fo + 1]))
                    p.wait(t, ev)
                up_done = p.tok("S")
                p.wait(t, up_done)
                p.wait(g, p.tok("T"))
                wt = {0: stream_w(wdn[e, :, :], 32, 0, WB[0])}
                prev_mmt = None
                for fo in range(8):
                    if fo + 1 < 8:
                        if prev_mmt is not None:
                            p.wait(g, prev_mmt)
                        wt[fo + 1] = stream_w(wdn[e, :, :], 32, fo + 1, WB[(fo + 1) % 2])
                    p.wait(t, wt[fo])
                    for n2 in range(2):
                        mm = None
                        for fi in range(32):
                            mm = t.matmul(ps[4 + n2][:, :], WB[fo % 2][:, fi, :],
                                          EH2[:, fi, 512 * n2:512 * n2 + 512],
                                          start=(fi == 0), stop=(fi == 31))
                        mmt = p.inc("T", mm)
                        prev_mmt = mmt
                        tsl = slice(1024 * hf + 512 * n2, 1024 * hf + 512 * n2 + 512)
                        cbt = p.inc("T", t.matmul(
                            ps[2 + n2][:, :], esel[:, e, :], cwT[0:EX, tsl],
                            start=True, stop=True))
                        p.wait(v, cbt)
                        a = p.inc("V", v.tensor_scalar(
                            scr5[:, n2, :], ps[4 + n2][:, :], bm[:, 32 + fo:33 + fo], None,
                            ALU.add))
                        p.wait(v, a)
                        a = p.inc("V", v.tensor_tensor(scr5[:, n2, :], scr5[:, n2, :],
                                                       ps[2 + n2][:, :], ALU.mult))
                        p.wait(v, a)
                        ev = p.inc("V", v.tensor_tensor(C[:, fo, tsl], C[:, fo, tsl],
                                                        scr5[:, n2, :], ALU.add))
                        p.wait(t, ev)
    moe_done = p.tok("V")

    p.wait(t, moe_done)
    fin = None
    for tt_ in range(NTT):
        lx = p.dma(g, "gx", xt128[:, :], x1scr[128 * tt_:128 * tt_ + 128, :])
        ev = None
        for c in range(NCH):
            bk = c % 2
            mm = p.inc("T", t.transpose(psb[bk][:, 0:128],
                                        C[:, c, 128 * tt_:128 * tt_ + 128], idn[:, :]))
            p.wait(v, mm)
            if c == 0:
                p.wait(v, lx)
            ev = p.inc("V", v.tensor_tensor(xt128[:, 128 * c:128 * c + 128],
                                            xt128[:, 128 * c:128 * c + 128],
                                            psb[bk][:, 0:128], ALU.add))
            p.wait(t, ev)
        p.wait(y, ev)
        fin = p.dma(y, "yd", x2o[128 * tt_:128 * tt_ + 128, :], xt128[:, :])
        p.wait(g, fin)
    p.wait(y, fin)
    st.close()
    return nc



def fm(vec):
    return np.ascontiguousarray(vec.reshape(-1, 128).T, dtype=np.float32)


def l3_consts():
    esel = np.zeros((EX, EX, 128), np.float32)
    for e in range(EX):
        esel[e, e, :] = 1.0
    return {
        "ones_col": np.ones((128, 1), np.float32),
        "ones_row": np.ones((1, 128), np.float32),
        "identity": np.eye(128, dtype=np.float32),
        "expsel": esel,
    }


def l3_weights(ins):
    w = dict(l3_consts())
    w["woT"] = np.ascontiguousarray(np.asarray(ins["L0_attn_w_out"]).T, np.float32)
    w["b_outF"] = fm(np.asarray(ins["L0_attn_b_out"]))
    w["f1T"] = np.ascontiguousarray(np.asarray(ins["L0_attn_w_ff1"]).T, np.float32)
    w["b_ff1F"] = fm(np.asarray(ins["L0_attn_b_ff1"]))
    w["f2T"] = np.ascontiguousarray(np.asarray(ins["L0_attn_w_ff2"]).T, np.float32)
    w["b_ff2F"] = fm(np.asarray(ins["L0_attn_b_ff2"]))
    w["ln1gF"] = fm(np.asarray(ins["L0_attn_ln1_g"]))
    w["ln1bF"] = fm(np.asarray(ins["L0_attn_ln1_b"]))
    w["ln2gF"] = fm(np.asarray(ins["L0_attn_ln2_g"]))
    w["ln2bF"] = fm(np.asarray(ins["L0_attn_ln2_b"]))
    w["Wg"] = np.ascontiguousarray(np.asarray(ins["L0_ffn_Wg"]), np.float32)
    w["Wup"] = np.ascontiguousarray(np.asarray(ins["L0_ffn_Wup"]), np.float32)
    w["bupF"] = np.stack([fm(np.asarray(ins["L0_ffn_bup"])[e]) for e in range(EX)])
    w["Wdown"] = np.ascontiguousarray(np.asarray(ins["L0_ffn_Wdown"]), np.float32)
    w["bdownF"] = np.stack([fm(np.asarray(ins["L0_ffn_bdown"])[e]) for e in range(EX)])
    return w


def kernel(**inputs):
    x = np.asarray(inputs["L0_attn_x"], np.float32)
    rotations = np.asarray(inputs["L0_attn_rotations"], np.float32)

    nc1 = build_L1()
    maps1 = l1_inputs(x, np.asarray(inputs["L0_attn_w_qk"], np.float32),
                      np.asarray(inputs["L0_attn_w_v"], np.float32), rotations)
    r1 = run_bass_kernel_spmd(nc1, maps1, core_ids=list(range(8)))
    o1 = r1.results

    qkT_b = [np.concatenate([np.asarray(o1[4 * b + k]["qkT"]) for k in range(4)],
                            axis=1) for b in range(B)]
    khT_b = [np.concatenate([np.asarray(o1[4 * b + k]["khT"]) for k in range(4)],
                            axis=1) for b in range(B)]
    vT_b = [np.concatenate([np.asarray(o1[4 * b + k]["vT"]) for k in range(4)],
                           axis=1) for b in range(B)]

    c2 = l2_consts()
    maps2 = []
    for j in range(8):
        cb, lj = divmod(j, 4)
        m = dict(c2)
        for ul in range(NLU):
            u = 3 * lj + ul
            h = NLOC + u
            qk_rows = np.ascontiguousarray(qkT_b[cb][64 * h:64 * h + 64, :].T)
            kh_rows = np.ascontiguousarray(khT_b[cb][64 * h:64 * h + 64, :].T)
            v_rows = np.ascontiguousarray(vT_b[cb][64 * h:64 * h + 64, :].T)
            m[f"src{ul}"] = l2_src(qk_rows, kh_rows, v_rows)
            m[f"buck{ul}"] = np.ascontiguousarray(np.concatenate(
                [np.asarray(o1[4 * cb + k]["buckO"])[:, u, :, :] for k in range(4)],
                axis=1))
        m["locq"] = np.ascontiguousarray(qkT_b[cb][64 * lj:64 * lj + 64, :])
        m["lock"] = np.ascontiguousarray(khT_b[cb][64 * lj:64 * lj + 64, :])
        vrows = np.asarray(vT_b[cb][64 * lj:64 * lj + 64, :]).T
        locv = np.zeros((128, 64, DH + 1), BF)
        locv[:, :, 0:DH] = vrows.reshape(64, 128, DH).transpose(1, 0, 2)
        locv[:, :, DH] = BF(1.0)
        m["locv"] = locv
        maps2.append(m)
    nc2 = build_L2()
    r2 = run_bass_kernel_spmd(nc2, maps2, core_ids=list(range(8)))
    o2 = r2.results

    w3 = l3_weights(inputs)
    loco_rows = {}
    for j in range(8):
        cb, lj = divmod(j, 4)
        lr = np.asarray(o2[j]["loco"])
        loco_rows[(cb, lj)] = lr.transpose(1, 0, 2).reshape(T, DH)
    maps3 = []
    for k in range(8):
        cb, kk = divmod(k, 4)
        sl = slice(TOKS * kk, TOKS * kk + TOKS)
        m = dict(w3)
        m["xT"] = np.ascontiguousarray(x[cb, sl, :].T)
        m["xrows"] = np.ascontiguousarray(x[cb, sl, :])
        m["locT"] = np.concatenate(
            [np.ascontiguousarray(loco_rows[(cb, h)][sl].T) for h in range(NLOC)],
            axis=0)
        pk = np.zeros((128, NLSH, R, TOKS // 128), np.float32)
        for u in range(NLSH):
            jc = 4 * cb + u // 3
            ul = u % 3
            m[f"oarr{u}"] = np.asarray(o2[jc][f"oarr{ul}"])
            pk[:, u, :, :] = np.asarray(o2[jc]["pirk"])[:, ul, :, 16 * kk:16 * kk + 16]
        m["pirk"] = pk
        maps3.append(m)
    nc3 = build_L3()
    r3 = run_bass_kernel_spmd(nc3, maps3, core_ids=list(range(8)))
    out = np.zeros((B, T, E), np.float32)
    for k in range(8):
        cb, kk = divmod(k, 4)
        out[cb, TOKS * kk:TOKS * kk + TOKS, :] = np.asarray(r3.results[k]["x2"])
    return out
